# version 01 (final)
# baseline (speedup 1.0000x reference)
import os
import numpy as np
import ml_dtypes
from contextlib import ExitStack

import concourse.bass as bass
import concourse.mybir as mybir
from concourse.bass_utils import run_bass_kernel_spmd

F32 = mybir.dt.float32
BF16 = mybir.dt.bfloat16
AF = mybir.ActivationFunctionType
ALU = mybir.AluOpType
AX = mybir.AxisListType.X

NCORE = 8
DM = 2048
NFC = 16
TW = 512
NTH = 2
HTOK = NTH * TW
TOK = 2 * HTOK
KVH = 4
QPG = 4
HD = 128

ABF_W = 26112
XN_O = 0
WQ_O, WQ_N = 2048, 4096
QF_O, QF_N = 10240, 2048
KFF_O = 14336
KFT_O = 14848
VT_O = 15360
MST_O, MST_N = 15872, 2048
QZ_O = 19968
WO_O, WO_N = 2048, 4096
WF_O, WF_N = 15872, 4096
TST_O = 10240
TSC_O = 12288
TSS_O = 12544
XNTP_O = 2048
KFP_O = 10240
VP_O = 10752
RECB_O = 22016
BCS_O, BCS_N = 23040, 512
RT1_O = 24064
RT2_O = 24576
EL1_O = 25088
EL2_O = 25600

AF32_W = 8448
CWDBG_O = 8288
XSF_O, XSF_N = 0, 2048
PTC_O = 4096
PTS_O = 6144
ZC_O, ZC_N = 4096, 512
RECF_O = 5120
XR_O, XR_N = 4096, 1024
YS_O, YS_N = 6144, 1024
STAT_O = 8192
ZRUN_O = 8200
GT_O = 8208

ENGINES = ("tensor", "vector", "scalar", "sync", "gpsimd")
OWN_SEM = {"tensor": "T", "vector": "V", "scalar": "S"}
SEMS = ("T", "V", "S", "SC", "GC", "TBL", "XS0", "XS1", "XR0", "XR1",
        "WQ0", "WQ1", "WS0", "WS1", "DY0", "DY1", "DD", "DDG")


class Sched:
    def __init__(self):
        self.ops = {e: [] for e in ENGINES}
        self.cnt = {s: 0 for s in SEMS}
        self.waited = {e: {} for e in ENGINES}
        self.last_kind = {e: None for e in ENGINES}

    def emit(self, eng, fn, waits=(), inc=None, amt=1, kind="op"):
        fl = []
        own = OWN_SEM.get(eng)
        for w in waits:
            if w is None:
                continue
            s, v = w
            if s == own:
                continue
            if v > self.waited[eng].get(s, 0):
                self.waited[eng][s] = v
                fl.append((s, v))
        if own is not None and fn is not None and kind != "dma":
            if SELFSYNC and self.cnt[own] > 0:
                fl.append((own, self.cnt[own]))
            self.cnt[own] += 1
            tok = (own, self.cnt[own])
            self.ops[eng].append((fl, fn, own, 1))
            return tok
        tok = None
        if inc:
            self.cnt[inc] += amt
            tok = (inc, self.cnt[inc])
        self.ops[eng].append((fl, fn, inc, amt))
        return tok


SELFSYNC = os.environ.get("KSELFSYNC", "0") == "1"
DRAIN_V = os.environ.get("KDRV", "1") == "1"
DRAIN_S = os.environ.get("KDRS", "1") == "1"
RACEDET = os.environ.get("KRACEDET", "0") == "1"


def build_program():
    phase = int(os.environ.get("KPHASE", "9"))
    kchunks = int(os.environ.get("KCHUNKS", "16"))
    kdump = int(os.environ.get("KDUMP", "1"))
    nc = bass.Bass("TRN2", target_bir_lowering=False,
                   detect_race_conditions=RACEDET)

    def din(name, shape, dt=F32):
        return nc.dram_tensor(name, shape, dt, kind="ExternalInput").ap()

    x_own = din("x_own", [TOK, DM])
    x_pref = din("x_pref", [TOK, DM])
    wqT = din("wqT", [128, 8 * NFC * 256])
    wkT = din("wkT", [128, NFC * 512])
    wvT = din("wvT", [128, NFC * 512])
    woT = din("woT", [128, 8 * NFC * 256])
    wfT = din("wfT", [128, 8 * NFC * 256])
    gwT = din("gwT", [128, NFC * 8])
    aall = din("aall", [128, NFC * 128])
    bcat = din("bcat", [128, 2048])
    cosT = din("cosT", [128, TOK])
    sinTs = din("sinTs", [128, TOK])
    cosP = din("cosP", [128, NFC * 128])
    sinPs = din("sinPs", [128, NFC * 128])
    triM = din("triM", [128, 896], BF16)
    ident = din("ident", [128, 128], BF16)
    onec = din("onec", [128, 1], BF16)
    oner = din("oner", [1, 128], BF16)
    cv = din("cv", [128, 2])
    flag = din("flag", [128, 1])
    y = nc.dram_tensor("y", [TOK, DM], F32, kind="ExternalOutput").ap()

    sch = Sched()

    with ExitStack() as ctx:
        sb = lambda name, shape, dt: ctx.enter_context(nc.sbuf_tensor(name, shape, dt))
        pst = lambda name, shape, dt: ctx.enter_context(nc.psum_tensor(name, shape, dt))

        wk_sb = sb("wk_sb", [128, NFC * 512], BF16)
        wv_sb = sb("wv_sb", [128, NFC * 512], BF16)
        aall_sb = sb("aall_sb", [128, NFC * 128], BF16)
        bcat_sb = sb("bcat_sb", [128, 2048], BF16)
        gw_sb = sb("gw_sb", [128, NFC * 8], BF16)
        cosT_sb = sb("cosT_sb", [128, HTOK], F32)
        sinT_sb = sb("sinT_sb", [128, HTOK], F32)
        tri_sb = sb("tri_sb", [128, 896], BF16)
        id_sb = sb("id_sb", [128, 128], BF16)
        onec_sb = sb("onec_sb", [128, 1], BF16)
        oner_sb = sb("oner_sb", [1, 128], BF16)
        cv_sb = sb("cv_sb", [128, 2], F32)
        flag_sb = sb("flag_sb", [128, 1], F32)
        st_a = sb("st_a", [128, 512], F32)
        st_b = sb("st_b", [128, 512], F32)
        st_bf = sb("st_bf", [128, 512], BF16)
        xnT = sb("xnT", [128, NFC * HTOK], BF16)
        attnT = sb("attnT", [128, 16 * HTOK], BF16)
        abf = sb("abf", [128, ABF_W], BF16)
        af32 = sb("af32", [128, AF32_W], F32)

        ps0 = pst("ps0", [128, 512], F32)
        ps1 = pst("ps1", [128, 512], F32)
        ps2 = pst("ps2", [128, 512], F32)
        ps3 = pst("ps3", [128, 512], F32)
        ps4 = pst("ps4", [128, 512], F32)
        ps5 = pst("ps5", [128, 512], F32)
        ps6 = pst("ps6", [128, 512], F32)
        ps7 = pst("ps7", [128, 512], F32)
        banks = {"ps0": ps0, "ps1": ps1, "ps2": ps2, "ps3": ps3,
                 "ps4": ps4, "ps5": ps5, "ps6": ps6, "ps7": ps7}

        sems = {s: ctx.enter_context(nc.semaphore(s)) for s in SEMS}

        def t_mm(out, lhsT, rhs, start, stop, waits=(), inc=False):
            return sch.emit("tensor",
                            lambda e, o=out, l=lhsT, r=rhs, s=start, p=stop:
                            e.matmul(o, l, r, start=s, stop=p),
                            waits, "T" if inc else None)

        gc_tok_box = [None]

        def t_tr(out, in_, waits=(), inc=False):
            return sch.emit("tensor",
                            lambda e, o=out, i=in_: e.matmul(o, i, id_sb[:],
                                                             start=True, stop=True),
                            list(waits) + [gc_tok_box[0]], "T" if inc else None)

        def v_op(fn, waits=(), inc=False, post_drain=True):
            tok = sch.emit("vector", fn, waits, "V" if inc else None)
            if post_drain and DRAIN_V:
                sch.emit("vector", lambda e: e.drain(), (), None)
            return tok

        def s_op(fn, waits=(), inc=False, post_drain=False):
            tok = sch.emit("scalar", fn, waits, "S" if inc else None)
            if post_drain and DRAIN_S:
                sch.emit("scalar", lambda e: e.drain(), (), None)
            return tok

        def dma(queue, semname, out, in_, waits=()):
            return sch.emit(queue,
                            lambda e, o=out, i=in_: e.dma_start(out=o, in_=i),
                            waits, semname, 16, kind="dma")

        def A(off, w, p0=0, pn=128):
            return abf[p0:p0 + pn, off:off + w]

        def A32(off, w, p0=0, pn=128):
            return af32[p0:p0 + pn, off:off + w]

        def Anest(off, dims):
            return bass.AP(abf, off, [[ABF_W, 128]] + dims)

        def A32nest(off, dims):
            return bass.AP(af32, off, [[AF32_W, 128]] + dims)

        dma("gpsimd", "GC", id_sb[:], ident)
        dma("gpsimd", "GC", onec_sb[:], onec)
        dma("gpsimd", "GC", oner_sb[:], oner)
        dma("gpsimd", "GC", tri_sb[:], triM)
        dma("gpsimd", "GC", gw_sb[:], gwT)
        dma("gpsimd", "GC", aall_sb[:], aall)
        dma("gpsimd", "GC", bcat_sb[:], bcat)
        dma("gpsimd", "GC", wk_sb[:], wkT)
        tok_gc = dma("gpsimd", "GC", wv_sb[:], wvT)
        dma("sync", "SC", cv_sb[:], cv)
        dma("sync", "SC", flag_sb[:], flag)
        dma("sync", "SC", A32nest(PTC_O, [[128, NFC], [1, 128]]), cosP)
        tok_sc = dma("sync", "SC", A32nest(PTS_O, [[128, NFC], [1, 128]]), sinPs)

        gc_tok_box[0] = tok_gc
        tok_id = tok_onec = tok_oner = tok_tri = tok_gw = tok_aall = None
        tok_bcat = tok_wk = tok_wv = tok_gc
        tok_cv = tok_flag = tok_cosP = tok_sinP = tok_sc
        tok_wk = tok_wv = tok_gw = tok_aall = tok_bcat = tok_gc
        tok_onec = tok_oner = tok_tri = tok_id = tok_gc
        stats_free = None
        xs_free = [None, None]
        bank_free = {k: None for k in banks}

        def ln_chunk(xsrc, par, dest_tensor, dest_base, dest_w, extra_ln_waits=()):
            nonlocal stats_free
            xsf = A32(XSF_O + par * XSF_N, 2048)
            dtok = dma("sync", "XS%d" % par, xsf, xsrc, waits=[xs_free[par]])
            tsum = s_op(lambda e, i=xsf: e.activation(
                A(XN_O, 2048), i, AF.Copy, accum_out=A32(STAT_O, 1)),
                waits=[dtok, stats_free], inc=True)
            tsq = v_op(lambda e, i=xsf: e.scalar_tensor_tensor(
                A(XN_O, 2048), i, 0.0, i, ALU.bypass, ALU.mult,
                accum_out=A32(STAT_O + 1, 1)),
                waits=[dtok, stats_free, tsum], inc=True, post_drain=True)
            v_op(lambda e: e.tensor_scalar(A32(STAT_O + 2, 1), A32(STAT_O, 1),
                                           1.0 / 2048.0, 0.0, ALU.mult, ALU.bypass),
                 waits=[tsum, tsq])
            v_op(lambda e: e.tensor_scalar(A32(STAT_O + 3, 1), A32(STAT_O + 1, 1),
                                           1.0 / 2048.0, 0.0, ALU.mult, ALU.bypass))
            v_op(lambda e: e.tensor_tensor(A32(STAT_O + 4, 1), A32(STAT_O + 2, 1),
                                           A32(STAT_O + 2, 1), ALU.mult))
            tvar = v_op(lambda e: e.tensor_tensor(A32(STAT_O + 5, 1), A32(STAT_O + 3, 1),
                                                  A32(STAT_O + 4, 1), ALU.subtract),
                        inc=True)
            s_op(lambda e: e.activation(A32(STAT_O + 6, 1), A32(STAT_O + 5, 1),
                                        AF.Ln, bias=cv_sb[:, 1:2]),
                 waits=[tvar, tok_cv] + list(extra_ln_waits), post_drain=True)
            trs = s_op(lambda e: e.activation(A32(STAT_O + 7, 1), A32(STAT_O + 6, 1),
                                              AF.Exp, scale=-0.5), inc=True)
            txn = v_op(lambda e, i=xsf: e.tensor_scalar(
                A(XN_O, 2048), i, A32(STAT_O + 2, 1), A32(STAT_O + 7, 1),
                ALU.subtract, ALU.mult), waits=[trs], inc=True)
            xs_free[par] = txn
            for bq in range(4):
                bk = "ps6" if bq % 2 == 0 else "ps5"
                ttr = None
                for k in range(4):
                    fc = bq * 4 + k
                    ttr = t_tr(banks[bk][:, k * 128:(k + 1) * 128],
                               A(XN_O + fc * 128, 128),
                               waits=[txn, bank_free[bk]], inc=(k == 3))
                dst = bass.AP(dest_tensor, dest_base + bq * 4 * dest_w,
                              [[dest_tensor.shape[1], 128], [dest_w, 4], [1, 128]])
                tev = s_op(lambda e, d=dst, b=banks[bk]: e.copy(d, b[:]),
                           waits=[ttr], inc=True)
                bank_free[bk] = tev
            stats_free = ("T", sch.cnt["T"])

        def rope_elu_fm(psb, tcol, dest, tgroup, tbl, dest_extra_waits=()):
            v_op(lambda e, p=psb, c=tcol: e.tensor_tensor(
                A(RT1_O, 512), p[:], cosT_sb[:, c:c + 512], ALU.mult),
                waits=[tgroup, tbl])
            v_op(lambda e, p=psb, c=tcol: e.tensor_tensor(
                A(RT2_O, 512, 0, 64), p[64:128, :], sinT_sb[0:64, c:c + 512], ALU.mult))
            tpf = v_op(lambda e, p=psb, c=tcol: e.tensor_tensor(
                A(RT2_O, 512, 64, 64), p[0:64, :], sinT_sb[64:128, c:c + 512], ALU.mult),
                inc=True)
            v_op(lambda e: e.tensor_tensor(A(RT1_O, 512), A(RT1_O, 512),
                                           A(RT2_O, 512), ALU.add))
            tmn = v_op(lambda e: e.tensor_scalar(A(EL1_O, 512), A(RT1_O, 512),
                                                 0.0, 0.0, ALU.min, ALU.bypass),
                       inc=True)
            tex = s_op(lambda e: e.activation(A(EL2_O, 512), A(EL1_O, 512), AF.Exp),
                       waits=[tmn], inc=True)
            tout = v_op(lambda e, d=dest: e.scalar_tensor_tensor(
                d, A(RT1_O, 512), 0.0, A(EL2_O, 512), ALU.max, ALU.add),
                waits=[tex] + list(dest_extra_waits), inc=True)
            return tout, tpf

        ZACC_O = ZRUN_O + 4
        t_stz = v_op(lambda e: e.memset(st_b[:], 0.0), inc=True)
        t_zz = v_op(lambda e: e.memset(A32(ZACC_O, 4), 0.0), inc=True)
        for pt in range(4):
            for c in range(4):
                tc = pt * 4 + c
                if tc >= kchunks:
                    continue
                ln_chunk(x_pref[tc * 128:(tc + 1) * 128, :], tc % 2,
                         abf, XNTP_O + c * 128, 512)
            txnt = ("S", sch.cnt["S"])
            txnt_v = ("V", sch.cnt["V"])
            for c in (range(4) if phase >= 1 else range(0)):
                tc = pt * 4 + c
                kb = "ps0" if c % 2 == 0 else "ps1"
                vb = "ps2" if c % 2 == 0 else "ps3"
                tk = tv = None
                for fc in range(NFC):
                    tk = t_mm(banks[kb][:],
                              A(XNTP_O + fc * 512 + c * 128, 128),
                              wk_sb[:, fc * 512: fc * 512 + 512],
                              start=(fc == 0), stop=(fc == NFC - 1),
                              waits=[txnt, txnt_v, tok_wk, bank_free[kb]],
                              inc=(fc == NFC - 1))
                for fc in range(NFC):
                    tv = t_mm(banks[vb][:],
                              A(XNTP_O + fc * 512 + c * 128, 128),
                              wv_sb[:, fc * 512: fc * 512 + 512],
                              start=(fc == 0), stop=(fc == NFC - 1),
                              waits=[txnt, txnt_v, tok_wv, bank_free[vb]],
                              inc=(fc == NFC - 1))
                pk = banks[kb]
                cos4 = A32nest(PTC_O + tc * 128, [[0, 4], [1, 128]])
                sinlo = A32nest(PTS_O + tc * 128, [[0, 4], [1, 64]])
                sinhi = A32nest(PTS_O + tc * 128 + 64, [[0, 4], [1, 64]])
                v_op(lambda e, p=pk, c4=cos4: e.tensor_tensor(
                    A(RT1_O, 512), p[:], c4, ALU.mult),
                    waits=[tk, tok_cosP, tok_sinP])
                rothi = bass.AP(pk, 64, [[512, 128], [128, 4], [1, 64]])
                rotlo = bass.AP(pk, 0, [[512, 128], [128, 4], [1, 64]])
                v_op(lambda e, r=rothi, s=sinlo: e.tensor_tensor(
                    Anest(RT2_O, [[128, 4], [1, 64]]), r, s, ALU.mult))
                tpf = v_op(lambda e, r=rotlo, s=sinhi: e.tensor_tensor(
                    Anest(RT2_O + 64, [[128, 4], [1, 64]]), r, s, ALU.mult), inc=True)
                bank_free[kb] = tpf
                v_op(lambda e: e.tensor_tensor(A(RT1_O, 512), A(RT1_O, 512),
                                               A(RT2_O, 512), ALU.add))
                tmn = v_op(lambda e: e.tensor_scalar(A(EL1_O, 512), A(RT1_O, 512),
                                                     0.0, 0.0, ALU.min, ALU.bypass),
                           inc=True)
                tex = s_op(lambda e: e.activation(A(EL2_O, 512), A(EL1_O, 512), AF.Exp),
                           waits=[tmn], inc=True)
                tkf = v_op(lambda e: e.scalar_tensor_tensor(
                    A(KFP_O, 512), A(RT1_O, 512), 0.0, A(EL2_O, 512),
                    ALU.max, ALU.add), waits=[tex], inc=True)
                tvp = v_op(lambda e, p=banks[vb]: e.tensor_copy(A(VP_O, 512), p[:]),
                           waits=[tv], inc=True)
                bank_free[vb] = tvp
                tst = tz = None
                for g in range(KVH):
                    tst = t_mm(ps4[:, g * 128:(g + 1) * 128],
                               A(KFP_O + g * 128, 128), A(VP_O + g * 128, 128),
                               start=(g == 0), stop=(g == KVH - 1),
                               waits=[tkf, tvp, bank_free["ps4"]],
                               inc=(g == KVH - 1))
                for g in range(KVH):
                    tz = t_mm(ps7[:, g:g + 1],
                              A(KFP_O + g * 128, 128), onec_sb[:],
                              start=(g == 0), stop=(g == KVH - 1),
                              waits=[tkf, tok_onec, bank_free["ps7"]],
                              inc=(g == KVH - 1))
                t_stz = v_op(lambda e: e.tensor_tensor(st_b[:], ps4[:], st_b[:],
                                                       ALU.add),
                             waits=[tst, t_stz], inc=True)
                bank_free["ps4"] = t_stz
                t_zz = v_op(lambda e: e.tensor_tensor(A32(ZACC_O, 4), ps7[:, 0:4],
                                                      A32(ZACC_O, 4), ALU.add),
                            waits=[tz, t_zz], inc=True)
                bank_free["ps7"] = t_zz
        pref_T = ("T", sch.cnt["T"])
        if phase < 1:
            t_sta = v_op(lambda e: e.memset(st_a[:], 0.0),
                         waits=[pref_T, tok_flag], inc=True)
            v_op(lambda e: e.memset(st_bf[:], 0.0), inc=True)
            t_zr = v_op(lambda e: e.memset(A32(ZRUN_O, 8), 0.0), inc=True)
        else:
            t_sta = v_op(lambda e: e.tensor_scalar(st_a[:], st_b[:], flag_sb[:, 0:1],
                                                   0.0, ALU.mult, ALU.bypass),
                         waits=[t_stz, tok_flag], inc=True)
            t_zr = v_op(lambda e: e.tensor_scalar(A32(ZRUN_O, 4), A32(ZACC_O, 4),
                                                  flag_sb[:, 0:1], 0.0, ALU.mult,
                                                  ALU.bypass), waits=[t_zz], inc=True)
        t_stbf = v_op(lambda e: e.tensor_copy(st_bf[:], st_a[:]), inc=True)
        bank_free["ps4"] = t_sta
        bank_free["ps7"] = t_zr

        st_cur, st_nxt = st_a, st_b
        t_state_bf = t_stbf
        t_state_cur = t_sta
        zrun_tok = t_zr
        qf_free = [None, None]
        mst_free = [None, None]
        bcs_free = [None, None]
        recb_free = [None, None]
        kff_free = None
        kft_free = None
        vt_free = None
        wq_tok = [None, None]
        wq_free = [pref_T, pref_T]
        wo_tok = [None, None]
        wf_tok = [None, None]
        wo_free = [None, None]
        wf_free = [None, None]
        tbl_free = None
        ydma_cnt = 0
        ys_free = [None, None]
        xr_free = [None, None]
        zc_guard = None
        zc_guard2 = None
        tsc_free = None
        gate_free = None

        for half in (range(2) if phase >= 9 else range(1) if phase >= 2 else range(0)):
            dma("sync", "TBL", cosT_sb[:],
                bass.AP(cosT.tensor, half * HTOK, [[TOK, 128], [1, HTOK]]),
                waits=[tbl_free])
            tok_tbl = dma("sync", "TBL", sinT_sb[:],
                          bass.AP(sinTs.tensor, half * HTOK, [[TOK, 128], [1, HTOK]]),
                          waits=[tbl_free])

            def load_wq(kvh, sub, extra=()):
                par = (kvh * 2 + sub) % 2
                qd0 = kvh * 512 + sub * 256
                sidx = kvh * 2 + sub
                wq_tok[par] = dma(
                    "gpsimd", "WQ%d" % par,
                    A(WQ_O + par * WQ_N, 4096),
                    bass.AP(wqT.tensor, sidx * 4096, [[8 * 4096, 128], [1, 4096]]),
                    waits=[wq_free[par]] + list(extra))

            load_wq(0, 0)
            load_wq(0, 1)

            for kvh in (range(KVH) if phase >= 3 else range(1)):
                for tl in range(NTH):
                    t_g = half * NTH + tl
                    par_t = tl % 2
                    if kvh == 0:
                        for c in range(4):
                            tc = t_g * 4 + c
                            ln_chunk(x_own[tc * 128:(tc + 1) * 128, :], tc % 2,
                                     xnT, tl * 512 + c * 128, HTOK)
                        txnt = ("S", sch.cnt["S"])
                        txnt_v = ("V", sch.cnt["V"])
                    else:
                        txnt = txnt_v = None
                    tqf_all = None
                    for qh in range(QPG):
                        sub = qh // 2
                        par_w = (kvh * 2 + sub) % 2
                        qb = "ps0" if qh % 2 == 0 else "ps1"
                        base = WQ_O + par_w * WQ_N + (qh % 2) * 128
                        tq = None
                        for fc in range(NFC):
                            tq = t_mm(
                                banks[qb][:],
                                A(base + fc * 256, 128),
                                xnT[:, fc * HTOK + tl * 512: fc * HTOK + tl * 512 + 512],
                                start=(fc == 0), stop=(fc == NFC - 1),
                                waits=[txnt, txnt_v, wq_tok[par_w],
                                       bank_free[qb], qf_free[par_t]],
                                inc=(fc == NFC - 1))
                        if tl == NTH - 1 and qh % 2 == 1:
                            wq_free[par_w] = tq
                        dest = A(QF_O + par_t * QF_N + qh * 512, 512)
                        tout, tpf = rope_elu_fm(banks[qb], tl * 512, dest, tq, tok_tbl)
                        bank_free[qb] = tpf
                        if qh == QPG - 1:
                            tqf_all = tout
                    if tl == NTH - 1 and kvh < KVH - 1:
                        load_wq(kvh + 1, 0)
                        load_wq(kvh + 1, 1)
                    tk = None
                    for fc in range(NFC):
                        tk = t_mm(
                            ps2[:],
                            wk_sb[:, fc * 512 + kvh * 128: fc * 512 + kvh * 128 + 128],
                            xnT[:, fc * HTOK + tl * 512: fc * HTOK + tl * 512 + 512],
                            start=(fc == 0), stop=(fc == NFC - 1),
                            waits=[txnt, txnt_v, tok_wk, bank_free["ps2"],
                                   kff_free],
                            inc=(fc == NFC - 1))
                    tkf, tkpf = rope_elu_fm(ps2, tl * 512, A(KFF_O, 512), tk, tok_tbl)
                    bank_free["ps2"] = tkpf
                    tzc = v_op(lambda e, p=par_t, k=kvh: e.tensor_tensor_scan(
                        A32(ZC_O + p * ZC_N, 512), A(KFF_O, 512), A(KFF_O, 512),
                        A32(ZRUN_O + k, 1), ALU.add, ALU.bypass),
                        waits=[tkf, zrun_tok, zc_guard, zc_guard2], inc=True,
                        post_drain=True)
                    zrun_tok = v_op(lambda e, p=par_t, k=kvh: e.tensor_copy(
                        A32(ZRUN_O + k, 1), A32(ZC_O + p * ZC_N + 511, 1)), inc=True)
                    ttr = None
                    for jj in range(4):
                        ttr = t_tr(ps6[:, jj * 128:(jj + 1) * 128],
                                   A(KFF_O + jj * 128, 128),
                                   waits=[tkf, bank_free["ps6"], kft_free],
                                   inc=(jj == 3))
                    tkft = s_op(lambda e: e.copy(A(KFT_O, 512), ps6[:]),
                                waits=[ttr], inc=True)
                    bank_free["ps6"] = tkft
                    tv = None
                    for c in range(4):
                        for fc in range(NFC):
                            tv = t_mm(
                                ps3[:, c * 128:(c + 1) * 128],
                                xnT[:, fc * HTOK + tl * 512 + c * 128:
                                    fc * HTOK + tl * 512 + c * 128 + 128],
                                wv_sb[:, fc * 512 + kvh * 128:
                                      fc * 512 + kvh * 128 + 128],
                                start=(fc == 0), stop=(fc == NFC - 1),
                                waits=[txnt, txnt_v, tok_wv,
                                       bank_free["ps3"], vt_free],
                                inc=(c == 3 and fc == NFC - 1))
                    tvt = v_op(lambda e: e.tensor_copy(A(VT_O, 512), ps3[:]),
                               waits=[tv], inc=True)
                    bank_free["ps3"] = tvt
                    for qh in range(QPG):
                        par_q = qh % 2
                        tqz = v_op(lambda e, q=qh, p=par_t: e.tensor_tensor(
                            A(QZ_O + q * 512, 512),
                            A(QF_O + p * QF_N + q * 512, 512),
                            A32(ZC_O + p * ZC_N, 512), ALU.mult),
                            waits=[tqf_all, tzc], inc=True)
                        mst_evs = []
                        for jj in range(4):
                            sbk = "ps0" if jj % 2 == 0 else "ps1"
                            n = 512 - jj * 128
                            tsc_mm = t_mm(
                                banks[sbk][:, jj * 128: 512],
                                A(KFF_O + jj * 128, 128),
                                A(QF_O + par_t * QF_N + qh * 512 + jj * 128, n),
                                start=True, stop=True,
                                waits=[tkf, tqf_all, bank_free[sbk],
                                       mst_free[par_q]],
                                inc=True)
                            tdg = v_op(
                                lambda e, b=banks[sbk], j=jj, p=par_q: e.tensor_tensor(
                                    A(MST_O + p * MST_N + j * 512 + j * 128, 128),
                                    b[:, j * 128: j * 128 + 128],
                                    tri_sb[:, 384:512], ALU.mult),
                                waits=[tsc_mm, tok_tri], inc=True)
                            mst_evs.append(tdg)
                            if jj < 3:
                                tod = s_op(
                                    lambda e, b=banks[sbk], j=jj, p=par_q: e.copy(
                                        A(MST_O + p * MST_N + j * 512 + (j + 1) * 128,
                                          512 - (j + 1) * 128),
                                        b[:, (j + 1) * 128: 512]),
                                    waits=[tsc_mm], inc=True)
                                bank_free[sbk] = tod
                                mst_evs.append(tod)
                            else:
                                bank_free[sbk] = tdg
                        nbk = "ps4" if par_q == 0 else "ps5"
                        t_mm(banks[nbk][:],
                             st_bf[:, kvh * 128:(kvh + 1) * 128],
                             A(QF_O + par_t * QF_N + qh * 512, 512),
                             start=True, stop=False,
                             waits=[t_state_bf, tqf_all, bank_free[nbk]])
                        tn = None
                        for jj in range(4):
                            n = 512 - jj * 128
                            tn = t_mm(
                                bass.AP(banks[nbk], jj * 128, [[512, 128], [1, n]]),
                                A(VT_O + jj * 128, 128),
                                A(MST_O + par_q * MST_N + jj * 512 + jj * 128, n),
                                start=False, stop=(jj == 3),
                                waits=[tvt] + mst_evs, inc=(jj == 3))
                        drow = 32 * par_q
                        tden = t_mm(ps7[drow:drow + 1, :], onec_sb[:],
                                    A(QZ_O + qh * 512, 512), start=True, stop=True,
                                    waits=[tqz, bank_free["ps7"], recb_free[par_q],
                                           tok_gc], inc=True)
                        s_op(lambda e, r=drow: e.activation(
                            A32(RECF_O, 512, r, 1), ps7[r:r + 1, :], AF.Ln,
                            bias=cv_sb[0:1, 0:1]), waits=[tden], post_drain=True)
                        trec = s_op(lambda e, r=drow, p=par_q: e.activation(
                            A(RECB_O + p * 512, 512, 0, 1), A32(RECF_O, 512, r, 1),
                            AF.Exp, scale=-1.0), inc=True)
                        bank_free["ps7"] = trec
                        bbk = "ps5" if par_q == 0 else "ps4"
                        tbc = t_mm(banks[bbk][:], oner_sb[:],
                                   A(RECB_O + par_q * 512, 512, 0, 1),
                                   start=True, stop=True,
                                   waits=[trec, tok_oner, bank_free[bbk]], inc=True)
                        recb_free[par_q] = tbc
                        tbcs = s_op(lambda e, b=banks[bbk], p=par_q: e.copy(
                            A(BCS_O + p * BCS_N, 512), b[:]),
                            waits=[tbc, bcs_free[par_q]], inc=True)
                        bank_free[bbk] = tbcs
                        qhg = kvh * QPG + qh
                        tat = v_op(lambda e, q=qhg, tt=tl, p=par_q, nb=banks[nbk]:
                                   e.tensor_tensor(
                                       attnT[:, q * HTOK + tt * 512:
                                             q * HTOK + tt * 512 + 512],
                                       nb[:], A(BCS_O + p * BCS_N, 512), ALU.mult),
                                   waits=[tn, tbcs], inc=True)
                        bank_free[nbk] = tat
                        bcs_free[par_q] = tat
                        mst_free[par_q] = tn
                    qf_free[par_t] = ("T", sch.cnt["T"])
                    kff_free = ("T", sch.cnt["T"])
                    tdl = None
                    for jj in range(4):
                        tdl = t_mm(ps2[:, 0:128],
                                   A(KFT_O + jj * 128, 128), A(VT_O + jj * 128, 128),
                                   start=(jj == 0), stop=(jj == 3),
                                   waits=[tkft, tvt, bank_free["ps2"]], inc=(jj == 3))
                    kft_free = tdl
                    vt_free = tdl
                    tsu = v_op(lambda e, sc=st_cur, sn=st_nxt, k=kvh: e.tensor_tensor(
                        sn[:, k * 128:(k + 1) * 128], ps2[:, 0:128],
                        sc[:, k * 128:(k + 1) * 128], ALU.add),
                        waits=[tdl, t_state_cur], inc=True)
                    t_state_bf = v_op(lambda e, sn=st_nxt, k=kvh: e.tensor_copy(
                        st_bf[:, k * 128:(k + 1) * 128],
                        sn[:, k * 128:(k + 1) * 128]), inc=True)
                    bank_free["ps2"] = tsu
                    t_state_cur = tsu
                    st_cur, st_nxt = st_nxt, st_cur
            passA_T = ("T", sch.cnt["T"])
            passA_V = ("V", sch.cnt["V"])
            passA_S = ("S", sch.cnt["S"])
            tbl_free = passA_V

            if phase < 4:
                break
            G = GT_O
            tst_toks = []
            for lc in range(8):
                tc = half * 8 + lc
                tg = None
                for fc in range(NFC):
                    tg = t_mm(ps7[:, 0:8],
                              xnT[:, fc * HTOK + lc * 128: fc * HTOK + lc * 128 + 128],
                              gw_sb[:, fc * 8: fc * 8 + 8],
                              start=(fc == 0), stop=(fc == NFC - 1),
                              waits=[tok_gw, bank_free["ps7"]], inc=(fc == NFC - 1))
                tm1 = v_op(lambda e: e.reduce_max(A32(G + 64, 1), ps7[:, 0:8], axis=AX),
                           waits=[tg, gate_free], inc=True, post_drain=True)
                tm1n = v_op(lambda e: e.tensor_scalar(
                    A32(G + 65, 1), A32(G + 64, 1), -1.0, 0.0, ALU.mult, ALU.bypass),
                    inc=True)
                te = s_op(lambda e: e.activation(A32(G, 8), ps7[:, 0:8], AF.Exp,
                                                 bias=A32(G + 65, 1)),
                          waits=[tm1n], inc=True)
                bank_free["ps7"] = te
                v_op(lambda e: e.tensor_scalar(A32(G + 24, 8), A32(G, 8), 1.0, 0.0,
                                               ALU.is_ge, ALU.bypass), waits=[te])
                v_op(lambda e: e.tensor_scalar(A32(G + 8, 8), A32(G + 24, 8),
                                               -1.0, 1.0, ALU.mult, ALU.add))
                v_op(lambda e: e.tensor_tensor(A32(G + 16, 8), A32(G, 8),
                                               A32(G + 8, 8), ALU.mult))
                v_op(lambda e: e.reduce_max(A32(G + 66, 1), A32(G + 16, 8), axis=AX),
                     post_drain=True)
                v_op(lambda e: e.tensor_scalar(A32(G + 32, 8), A32(G + 16, 8),
                                               A32(G + 66, 1), 0.0, ALU.is_ge,
                                               ALU.bypass))
                v_op(lambda e: e.tensor_tensor(A32(G + 40, 8), A32(G + 24, 8),
                                               A32(G + 32, 8), ALU.add))
                v_op(lambda e: e.tensor_scalar(A32(G + 67, 1), A32(G + 66, 1),
                                               1.0, 0.0, ALU.add, ALU.bypass))
                v_op(lambda e: e.reciprocal(A32(G + 68, 1), A32(G + 67, 1)),
                     post_drain=True)
                v_op(lambda e: e.tensor_scalar(A32(G + 69, 1), A32(G + 68, 1),
                                               16.0, 0.0, ALU.mult, ALU.bypass))
                v_op(lambda e: e.tensor_tensor(A32(G + 48, 8), A32(G, 8),
                                               A32(G + 40, 8), ALU.mult))
                tcw = v_op(lambda e: e.tensor_scalar(A32(G + 56, 8), A32(G + 48, 8),
                                                     A32(G + 69, 1), 0.0,
                                                     ALU.mult, ALU.bypass), inc=True)
                gate_free = tcw
                if phase < 9:
                    v_op(lambda e, c=lc: e.tensor_copy(A32(CWDBG_O + c * 8, 8),
                                                       A32(G + 56, 8)))
                tb = "ps2" if lc % 2 == 0 else "ps3"
                tt = None
                for fc in range(NFC):
                    tt = t_mm(banks[tb][:, 0:128],
                              xnT[:, fc * HTOK + lc * 128: fc * HTOK + lc * 128 + 128],
                              aall_sb[:, fc * 128: fc * 128 + 128],
                              start=(fc == 0), stop=(fc == NFC - 1),
                              waits=[tok_aall, bank_free[tb]], inc=(fc == NFC - 1))
                tev = s_op(lambda e, b=banks[tb]: e.copy(A(TSC_O, 128), b[:, 0:128]),
                           waits=[tt, tsc_free], inc=True)
                bank_free[tb] = tev
                tss = None
                for ex in range(8):
                    tss = v_op(lambda e, x=ex: e.tensor_scalar(
                        A(TSS_O + x * 16, 16), A(TSC_O + x * 16, 16),
                        A32(G + 56 + x, 1), 0.0, ALU.mult, ALU.bypass),
                        waits=[tev, tcw], inc=(ex == 7))
                ttr2 = t_tr(ps6[:, 0:128], A(TSS_O, 128),
                            waits=[tss, bank_free["ps6"]], inc=True)
                tte = s_op(lambda e, c=lc: e.copy(
                    abf[:, TST_O + c * 128: TST_O + c * 128 + 128], ps6[:, 0:128]),
                    waits=[ttr2], inc=True)
                bank_free["ps6"] = tte
                tsc_free = tte
                tst_toks.append(tte)

            def load_wslice(s):
                par = s % 2
                dma("gpsimd", "WS%d" % par,
                    A(WO_O + par * WO_N, 4096),
                    bass.AP(woT.tensor, s * 4096, [[8 * 4096, 128], [1, 4096]]),
                    waits=[wo_free[par], wf_free[par], wq_free[0], wq_free[1],
                           passA_T, passA_V])
                wo_tok[par] = wf_tok[par] = dma(
                    "gpsimd", "WS%d" % par,
                    A(WF_O + par * WF_N, 4096),
                    bass.AP(wfT.tensor, s * 4096, [[8 * 4096, 128], [1, 4096]]))

            load_wslice(0)
            load_wslice(1)

            last_y_mm = None
            for s in range(8):
                par_s = s % 2
                for g in range(NTH):
                    par_g = g % 2
                    g_g = half * NTH + g
                    xr_tok = dma(
                        "sync", "XR%d" % par_g,
                        A32nest(XR_O + par_g * XR_N, [[256, 4], [1, 256]]),
                        bass.AP(x_own.tensor, g_g * 512 * DM + s * 256,
                                [[DM, 128], [128 * DM, 4], [1, 256]]),
                        waits=[xr_free[par_g], passA_V, passA_S])
                    tys = []
                    for c in range(4):
                        lc = g * 4 + c
                        yb = "ps0" if lc % 2 == 0 else "ps1"
                        for fc in range(NFC):
                            t_mm(banks[yb][:, 0:256],
                                 attnT[:, fc * HTOK + lc * 128:
                                       fc * HTOK + lc * 128 + 128],
                                 A(WO_O + par_s * WO_N + fc * 256, 256),
                                 start=(fc == 0), stop=False,
                                 waits=[wo_tok[par_s], bank_free[yb], passA_V])
                        for fc in range(NFC):
                            t_mm(banks[yb][:, 0:256],
                                 xnT[:, fc * HTOK + lc * 128:
                                     fc * HTOK + lc * 128 + 128],
                                 A(WF_O + par_s * WF_N + fc * 256, 256),
                                 start=False, stop=False,
                                 waits=[wf_tok[par_s]])
                        tmm = t_mm(banks[yb][:, 0:256],
                                   abf[:, TST_O + lc * 128: TST_O + lc * 128 + 128],
                                   bcat_sb[:, s * 256: s * 256 + 256],
                                   start=False, stop=True,
                                   waits=[tst_toks[lc], tok_bcat], inc=True)
                        last_y_mm = tmm
                        ty = v_op(lambda e, b=banks[yb], pg=par_g, cc=c:
                                  e.tensor_tensor(
                                      A32(YS_O + pg * YS_N + cc * 256, 256),
                                      b[:, 0:256],
                                      A32(XR_O + pg * XR_N + cc * 256, 256), ALU.add),
                                  waits=[tmm, xr_tok, ys_free[par_g]], inc=True)
                        bank_free[yb] = ty
                        tys.append(ty)
                    xr_free[par_g] = tys[-1]
                    tyd = dma("scalar", "DY%d" % par_g,
                              bass.AP(y.tensor, g_g * 512 * DM + s * 256,
                                      [[DM, 128], [128 * DM, 4], [1, 256]]),
                              A32nest(YS_O + par_g * YS_N, [[256, 4], [1, 256]]),
                              waits=tys)
                    ys_free[par_g] = tyd
                    ydma_cnt += 1
                wo_free[par_s] = last_y_mm
                wf_free[par_s] = last_y_mm
                if s + 2 < 8:
                    load_wslice(s + 2)
            zc_guard = ("DY0", sch.cnt["DY0"])
            zc_guard2 = ("DY1", sch.cnt["DY1"])
            wq_free = [last_y_mm, last_y_mm]

        if phase >= 9:
            sch.emit("scalar", None, [("DY0", sch.cnt["DY0"]),
                                      ("DY1", sch.cnt["DY1"])], None)
        else:
            allw = [("T", sch.cnt["T"]), ("V", sch.cnt["V"]), ("S", sch.cnt["S"])]
            nd = 0
            if phase < 4 and kdump:
                dma("sync", "DD", y[0:128, 0:512], st_a[:], waits=allw)
                dma("sync", "DD", y[128:256, 0:4], A32(ZRUN_O, 4), waits=allw)
                dma("gpsimd", "DDG", y[256:384, 0:2048], A(XNTP_O, 2048), waits=allw)
                dma("gpsimd", "DDG", y[640:768, 0:512], st_bf[:], waits=allw)
                dma("gpsimd", "DDG", y[768:896, 0:512], A(KFP_O, 512), waits=allw)
                dma("gpsimd", "DDG", y[896:1024, 0:512], A(VP_O, 512), waits=allw)
                dma("gpsimd", "DDG", y[1024:1152, 0:512], A(RT1_O, 512), waits=allw)
                dma("gpsimd", "DDG", y[1152:1280, 0:512], A(EL2_O, 512), waits=allw)
                dma("sync", "DD", y[1280:1408, 0:1024], A32(ZC_O, 1024), waits=allw)
                dma("sync", "DD", y[1536:1664, 0:128], A32(CWDBG_O, 128), waits=allw)
                dma("gpsimd", "DDG", y[1664:1792, 0:1024], A(TST_O, 1024), waits=allw)
                dma("gpsimd", "DDG", y[1408:1536, 0:512], A(QZ_O, 512), waits=allw)
                if phase >= 2:
                    dma("gpsimd", "DDG", y[384:512, 0:2048], xnT[:, 0:2048], waits=allw)
                    dma("gpsimd", "DDG", y[512:640, 0:2048], attnT[:, 0:2048], waits=allw)
            sch.emit("scalar", None, [("DD", sch.cnt["DD"]),
                                      ("DDG", sch.cnt["DDG"]),
                                      ("DY0", sch.cnt["DY0"]),
                                      ("DY1", sch.cnt["DY1"])], None)

        blk = ctx.enter_context(nc.Block())
        for ename in ENGINES:
            ops = sch.ops[ename]
            if not ops:
                continue

            def body(eng, _ops=ops):
                for waits, fn, inc, amt in _ops:
                    for (snm, v) in waits:
                        eng.wait_ge(sems[snm], v)
                    if fn is not None:
                        ins = fn(eng)
                        if inc:
                            ins.then_inc(sems[inc], amt)
            getattr(blk, ename)(body)

    return nc


def _prep_big(w):
    wT = w.T.reshape(NFC, 128, 8, 256)
    return np.ascontiguousarray(wT.transpose(1, 2, 0, 3).reshape(128, 8 * NFC * 256))


def _prep_kv(w):
    wT = w.T.reshape(NFC, 128, 512)
    return np.ascontiguousarray(wT.transpose(1, 0, 2).reshape(128, NFC * 512))


def _prep_ptab(t):
    return np.ascontiguousarray(t.reshape(NFC, 128, 128).transpose(1, 0, 2)
                                .reshape(128, NFC * 128))


def _host_tables():
    inv_freq = (1.0 / (np.float32(10000.0) **
                       (np.arange(0, HD, 2, dtype=np.float32) / np.float32(HD))))
    t = np.arange(4096, dtype=np.float32)
    freqs = np.outer(t, inv_freq).astype(np.float32)
    emb = np.concatenate([freqs, freqs], axis=-1)
    cos = np.cos(emb).astype(np.float32)
    sin = np.sin(emb).astype(np.float32)
    sin_signed = np.concatenate([-sin[:, :64], sin[:, 64:]], axis=1)
    return cos, sin_signed


_NC_CACHE = []


def _prep_inputs(L0_attn_x, L0_attn_Wq, L0_attn_Wk, L0_attn_Wv, L0_attn_Wo,
                 L0_ffn_weight, L0_ffn_gate_w, L0_ffn_A_w, L0_ffn_B_w):
    x = np.asarray(L0_attn_x, dtype=np.float32)
    wqT = _prep_big(np.asarray(L0_attn_Wq, np.float32))
    woT = _prep_big(np.asarray(L0_attn_Wo, np.float32))
    wfT = _prep_big(np.asarray(L0_ffn_weight, np.float32))
    wkT = _prep_kv(np.asarray(L0_attn_Wk, np.float32))
    wvT = _prep_kv(np.asarray(L0_attn_Wv, np.float32))
    gwT = np.ascontiguousarray(
        np.asarray(L0_ffn_gate_w, np.float32).T.reshape(NFC, 128, 8)
        .transpose(1, 0, 2).reshape(128, NFC * 8))
    aall = np.ascontiguousarray(
        np.asarray(L0_ffn_A_w, np.float32).transpose(2, 0, 1).reshape(DM, 128)
        .reshape(NFC, 128, 128).transpose(1, 0, 2).reshape(128, NFC * 128))
    bcat = np.ascontiguousarray(
        np.asarray(L0_ffn_B_w, np.float32).transpose(0, 2, 1).reshape(128, DM))

    cos, sins = _host_tables()
    jj = np.arange(128)[:, None]
    cc = np.arange(896)[None, :]
    triM = (jj <= cc - 384).astype(ml_dtypes.bfloat16)
    ident = np.eye(128, dtype=ml_dtypes.bfloat16)
    onec = np.ones((128, 1), ml_dtypes.bfloat16)
    oner = np.ones((1, 128), ml_dtypes.bfloat16)
    cvv = np.tile(np.array([[1e-6, 1e-5]], np.float32), (128, 1))

    in_maps = []
    for c in range(NCORE):
        b, h = c // 2, c % 2
        own = slice(h * TOK, (h + 1) * TOK)
        in_maps.append(dict(
            x_own=np.ascontiguousarray(x[b, own, :]),
            x_pref=np.ascontiguousarray(x[b, 0:TOK, :]),
            wqT=wqT, wkT=wkT, wvT=wvT, woT=woT, wfT=wfT,
            gwT=gwT, aall=aall, bcat=bcat,
            cosT=np.ascontiguousarray(cos[own, :].T),
            sinTs=np.ascontiguousarray(sins[own, :].T),
            cosP=_prep_ptab(cos[0:TOK, :]),
            sinPs=_prep_ptab(sins[0:TOK, :]),
            triM=triM, ident=ident, onec=onec, oner=oner,
            cv=cvv, flag=np.full((128, 1), float(h), np.float32)))
    return in_maps


def kernel(**inputs):
    in_maps = _prep_inputs(**inputs)
    nc = build_program()
    res = run_bass_kernel_spmd(nc, in_maps, core_ids=list(range(NCORE)))
    out = np.empty((4, 4096, DM), np.float32)
    for c in range(NCORE):
        b, h = c // 2, c % 2
        out[b, h * TOK:(h + 1) * TOK, :] = res.results[c]["y"]
    return out


if __name__ == "__main__":
    import reference as R
    ins = R.setup_inputs()
    got = kernel(**{k: np.asarray(v) for k, v in ins.items()})
    want = np.asarray(R.reference(**ins))
    num = np.mean((got - want) ** 2)
    den = np.mean(np.asarray(want) ** 2)
    print("ratio:", num / den)
